# version 16
# baseline (speedup 1.0000x reference)
import math
import contextlib
import numpy as np

import concourse.bass as bass
import concourse.mybir as mybir
from concourse.bass_utils import run_bass_kernel_spmd

F32 = mybir.dt.float32
BF16 = mybir.dt.bfloat16
AF = mybir.ActivationFunctionType
ALU = mybir.AluOpType
AX = mybir.AxisListType

NCORES = 8
S = 4096
D = 1024
H = 16
HD = 64
WIN = 512
FF = 4096
NE = 8
EPS = 1e-5
TILES = 32
NCH = 4
SCALE = 1.0 / 32.0

ALIBI = [1.0 / ((2.0 ** 8) ** (1.0 / 16)) ** (i + 1) for i in range(H)]

A_K = 0
A_V = 8192
A_Q = 16384
A_LX1 = 20480
A_KQP = 24576
A_VP = 26624
A_PT = 34816
A_XE = 0
A_W1 = 8192
A_H = 20480
A_W2 = 36864
ARENA_COLS = 49152

PS_ROT = (0, 512)
SREG = (0, 1024, 2048)
PS_RS = 3072
PS_PVC = (3104, 3168)
PS_AFF = 3232
PS_ONET = 3264
PS_T7 = 3584
PSO = {(0, 0): 1024, (0, 1): 1536, (1, 0): 2048, (1, 1): 2560,
       (2, 0): 3072, (2, 1): 3584, (3, 0): 0, (3, 1): 512}
PS_B2 = (1024, 1536)

ENGS = ("pe", "act", "dve", "pool", "sp")
SEMS = ("P", "A", "V", "G", "DG", "DS", "DO")
ESEM = {"pe": "P", "act": "A", "dve": "V", "pool": "G", "sp": None}


class Res:
    __slots__ = ("w", "r")

    def __init__(self, w=None):
        self.w = w
        self.r = []


class Sched:
    def __init__(self):
        self.q = {e: [] for e in ENGS}
        self.C = {s: 0 for s in SEMS}
        self.lastw = {e: {s: 0 for s in SEMS} for e in ENGS}

    def op(self, eng, fn, deps=(), inc=None, n=1):
        dd = []
        for d in deps:
            if d is None:
                continue
            s, v = d
            if v > self.lastw[eng][s]:
                dd.append((s, v))
                self.lastw[eng][s] = v
        mile = None
        if inc is not None:
            self.C[inc] += n
            mile = (inc, self.C[inc])
        self.q[eng].append((dd, fn, inc, n))
        return mile

    def eop(self, eng, fn, deps=()):
        return self.op(eng, fn, deps, inc=ESEM[eng], n=1)


def rpeek(*ress):
    best = {}
    for res in ress:
        for m in list(res.r) + ([res.w] if res.w is not None else []):
            s, v = m
            if v > best.get(s, -1):
                best[s] = v
    return list(best.items())


def rread(res, mile):
    if mile is not None:
        res.r.append(mile)
    return res.w


def rwrite(res, mile):
    deps = list(res.r)
    if res.w is not None:
        deps.append(res.w)
    res.r = []
    res.w = mile
    return deps


def build_program():
    nc = bass.Bass("TRN2", target_bir_lowering=False)
    x_d = nc.dram_tensor("x", [S, D], F32, kind="ExternalInput").ap()
    wkqv_d = nc.dram_tensor("wkqv", [24, 128, 1024], F32, kind="ExternalInput").ap()
    w1_d = nc.dram_tensor("w1", [NE, 8, 128, 4096], F32, kind="ExternalInput").ap()
    w2_d = nc.dram_tensor("w2", [NE, 8, 128, 4096], F32, kind="ExternalInput").ap()
    b1c_d = nc.dram_tensor("b1c", [128, 256], F32, kind="ExternalInput").ap()
    gc_d = nc.dram_tensor("gc", [128, 64], F32, kind="ExternalInput").ap()
    bc_d = nc.dram_tensor("bc", [128, 64], F32, kind="ExternalInput").ap()
    centc_d = nc.dram_tensor("centc", [128, 64], F32, kind="ExternalInput").ap()
    b2_d = nc.dram_tensor("b2", [NE, D], F32, kind="ExternalInput").ap()
    y_d = nc.dram_tensor("y", [S, D], F32, kind="ExternalOutput").ap()

    with contextlib.ExitStack() as es:
        E = es.enter_context
        arena = E(nc.sbuf_tensor("arena", [128, ARENA_COLS], BF16))
        resid = E(nc.sbuf_tensor("resid", [128, 8 * 1024], F32))
        xn3fm = E(nc.sbuf_tensor("xn3fm", [128, 8 * 1024], BF16))
        ktail = E(nc.sbuf_tensor("ktail", [128, 8 * 512], BF16))
        vtail = E(nc.sbuf_tensor("vtail", [128, 4 * 1024], BF16))
        lx1tm = E(nc.sbuf_tensor("lx1tm", [128, 1024], F32))
        lx2scr = E(nc.sbuf_tensor("lx2scr", [128, 1024], F32))
        xn3tm = E(nc.sbuf_tensor("xn3tm", [128, 1024], F32))
        xn3f32 = E(nc.sbuf_tensor("xn3f32", [128, 1024], F32))
        sstage = E(nc.sbuf_tensor("sstage", [128, 4 * 640], BF16))
        pTsb = E(nc.sbuf_tensor("pTsb", [128, 4 * 640], BF16))
        ones_b = E(nc.sbuf_tensor("ones_b", [128, 1], BF16))
        attnsb = E(nc.sbuf_tensor("attnsb", [128, 1024], F32))
        smx = E(nc.sbuf_tensor("smx", [128, 48], F32))
        amask = E(nc.sbuf_tensor("amask", [128, 64], F32))
        rel = E(nc.sbuf_tensor("rel", [128, 640], F32))
        identb = E(nc.sbuf_tensor("identb", [128, 128], BF16))
        identf = E(nc.sbuf_tensor("identf", [128, 128], F32))
        b1c = E(nc.sbuf_tensor("b1c_s", [128, 256], F32))
        gc = E(nc.sbuf_tensor("gc_s", [128, 64], F32))
        bc = E(nc.sbuf_tensor("bc_s", [128, 64], F32))
        centc = E(nc.sbuf_tensor("centc_s", [128, 64], F32))
        b2sb = E(nc.sbuf_tensor("b2sb", [NE, D], BF16))
        onehot = E(nc.sbuf_tensor("onehot", [128, 64], F32))
        oneT = E(nc.sbuf_tensor("oneT", [NE, 8 * 128], BF16))
        alpha = E(nc.sbuf_tensor("alpha", [128, 8], F32))
        stats = E(nc.sbuf_tensor("stats", [128, 96], F32))
        ones_c = E(nc.sbuf_tensor("ones_c", [128, 1], F32))
        ones_r = E(nc.sbuf_tensor("ones_r", [1, 128], F32))
        csr = E(nc.sbuf_tensor("csr", [8, 2], F32))
        csrow = E(nc.sbuf_tensor("csrow", [1, 8], F32))
        csumb = E(nc.sbuf_tensor("csumb", [128, 8], F32))
        psA = E(nc.psum_tensor("psA", [128, 4096], F32))
        sems = {s: E(nc.semaphore(s)) for s in SEMS}
        block = E(nc.Block())

        sc = Sched()
        bank = [Res() for _ in range(8)]

        def ps_w(mile_placeholder_cols):
            pass

        def ps_banks(c0, n):
            b0 = c0 // 512
            b1 = (c0 + n - 1) // 512
            return range(b0, b1 + 1)

        def ps_wdeps(c0, n):
            dd = []
            for b in ps_banks(c0, n):
                dd.extend(bank[b].r)
                if bank[b].w is not None:
                    dd.append(bank[b].w)
                bank[b].r = []
            return dd

        def ps_setw(c0, n, mile):
            for b in ps_banks(c0, n):
                bank[b].w = mile

        def ps_rdeps(c0, n):
            return [bank[b].w for b in ps_banks(c0, n) if bank[b].w is not None]

        def ps_setr(c0, n, mile):
            if mile is None:
                return
            for b in ps_banks(c0, n):
                bank[b].r.append(mile)

        R = {name: Res() for name in (
            "Kbuf", "Vbuf", "Qbuf", "Lx1fm", "VP", "ktail", "vtail",
            "xn3fm", "onehot", "oneT", "alpha",
            "lx1tm", "lx2scr", "xn3tm", "xn3f32", "hsb",
        )}
        for nm in (["kqp0", "kqp1", "xe0", "xe1", "w10", "w11", "w12", "w20", "w21", "w22",
                    "attnsb", "amask"]
                   + [f"sst{i}" for i in range(4)]
                   + [f"ptb{i}" for i in range(4)]):
            R[nm] = Res()
        prev_ev = [None] * H
        Rtile = [Res() for _ in range(8)]

        st_i = {"negs": 0, "negmu": 1, "v": 2, "ssq": 3, "nmr": 4, "mu": 5,
                "maxa": 6, "ealpha": 7, "talpha": 8}

        def stc(base, name):
            i = base + st_i[name]
            return stats[:, i:i + 1]

        m = sc.op("pool", lambda g: g.dma_start(out=b1c[:, :], in_=b1c_d), inc="DG", n=16)
        R["b1c"] = Res(m)
        m = sc.op("pool", lambda g: g.dma_start(out=gc[:, :], in_=gc_d), inc="DG", n=16)
        R["gc"] = Res(m)
        m = sc.op("pool", lambda g: g.dma_start(out=bc[:, :], in_=bc_d), inc="DG", n=16)
        R["bc"] = Res(m)
        m = sc.op("pool", lambda g: g.dma_start(out=centc[:, :], in_=centc_d), inc="DG", n=16)
        R["centc"] = Res(m)
        m = sc.op("pool", lambda g: g.dma_start(out=b2sb[:, :], in_=b2_d), inc="DG", n=16)
        R["b2sb"] = Res(m)

        def startup_pool(g):
            for w in range(5):
                cb = (w - 4) * 128
                g.iota(rel[:, w * 128:(w + 1) * 128], [[-1, 128]], base=cb,
                       channel_multiplier=1,
                       allow_small_or_imprecise_dtypes=True)
                g.drain()
                g.affine_select(out=rel[:, w * 128:(w + 1) * 128],
                                in_=rel[:, w * 128:(w + 1) * 128],
                                compare_op=ALU.is_ge, fill=-1.0e30, base=-cb,
                                pattern=[[1, 128]], channel_multiplier=-1)
                g.drain()
                g.affine_select(out=rel[:, w * 128:(w + 1) * 128],
                                in_=rel[:, w * 128:(w + 1) * 128],
                                compare_op=ALU.is_ge, fill=-1.0e30, base=cb + 512,
                                pattern=[[-1, 128]], channel_multiplier=1)
                g.drain()
            g.memset(ones_b[:, :], 1.0)
            g.memset(identb[:, :], 0.0)
            g.drain()
            g.affine_select(out=identb[:, :], in_=identb[:, :], compare_op=ALU.not_equal,
                            fill=1.0, base=0, pattern=[[-1, 128]], channel_multiplier=1)
            g.memset(identf[:, :], 0.0)
            g.drain()
            g.affine_select(out=identf[:, :], in_=identf[:, :], compare_op=ALU.not_equal,
                            fill=1.0, base=0, pattern=[[-1, 128]], channel_multiplier=1)
            g.memset(ones_c[:, :], 1.0)
            return g.memset(ones_r[:, :], 1.0)
        m_start = sc.eop("pool", startup_pool)
        R["static"] = Res(m_start)

        def csum_mms(t):
            ins = None
            for dc in range(8):
                ins = t.matmul(psA[0:8, 1024:1025], centc[:, dc * 8:(dc + 1) * 8],
                               ones_c[:, :], start=(dc == 0), stop=(dc == 7))
            return ins
        mm = sc.eop("pe", csum_mms, deps=[R["centc"].w, m_start] + ps_wdeps(1024, 1))
        ps_setw(1024, 1, mm)
        mcp = sc.eop("dve", lambda v: v.tensor_copy(csr[0:8, 0:1], psA[0:8, 1024:1025]),
                     deps=ps_rdeps(1024, 1))
        ps_setr(1024, 1, mcp)
        mtr = sc.eop("pe", lambda t: t.transpose(psA[0:1, 1152:1160], csr[0:8, 0:1],
                                                 identf[0:8, 0:8]),
                     deps=[mcp] + ps_wdeps(1152, 8))
        ps_setw(1152, 8, mtr)
        mcr = sc.eop("dve", lambda v: v.tensor_copy(csrow[0:1, :], psA[0:1, 1152:1160]),
                     deps=ps_rdeps(1152, 8))
        ps_setr(1152, 8, mcr)
        mbc = sc.eop("pe", lambda t: t.matmul(psA[:, 1160:1168], ones_r[0:1, :],
                                              csrow[0:1, :], start=True, stop=True),
                     deps=[mcr] + ps_wdeps(1160, 8))
        ps_setw(1160, 8, mbc)
        mcb = sc.eop("dve", lambda v: v.tensor_copy(csumb[:, :], psA[:, 1160:1168]),
                     deps=ps_rdeps(1160, 8))
        ps_setr(1160, 8, mcb)
        R["csumb"] = Res(mcb)

        def emit_ln(src_ap, src_dep, base, out_ap, dump_ap, need_mu_sd=False,
                    extra_out_deps=()):
            negs, negmu, v, nmr = (stc(base, "negs"), stc(base, "negmu"),
                                   stc(base, "v"), stc(base, "nmr"))
            ssq, mu = stc(base, "ssq"), stc(base, "mu")
            m1 = sc.eop("dve", lambda v_: v_.reduce_sum(negs, src_ap, axis=AX.X,
                                                        negate=True),
                        deps=[src_dep])
            sc.op("dve", lambda v_: v_.drain())
            m2 = sc.eop("dve", lambda v_: v_.tensor_scalar_mul(negmu, negs, 1.0 / D))
            if need_mu_sd:
                m_mu = sc.eop("dve", lambda v_: v_.tensor_scalar_mul(mu, negs, -1.0 / D))
            sc.op("dve", lambda v_: v_.drain())
            m3 = sc.eop("act", lambda a: a.activation(dump_ap, src_ap, AF.Square,
                                                      bias=negmu, scale=1.0,
                                                      accum_out=ssq),
                        deps=[m2, src_dep] + list(extra_out_deps))
            m4 = sc.eop("dve", lambda v_: v_.tensor_scalar(v, ssq, 1.0 / D, EPS,
                                                           ALU.mult, ALU.add),
                        deps=[m3])
            sc.op("dve", lambda v_: v_.drain())
            lnv = stc(base, "ealpha")
            m5 = sc.eop("act", lambda a: a.activation(lnv, v, AF.Ln), deps=[m4])
            rstd = stc(base, "talpha")
            m6 = sc.eop("act", lambda a: a.activation(rstd, lnv, AF.Exp, scale=-0.5))
            msd = None
            sd = stc(base, "maxa")
            if need_mu_sd:
                msd = sc.eop("act", lambda a: a.activation(sd, lnv, AF.Exp, scale=0.5))
            m7 = sc.eop("dve", lambda v_: v_.tensor_mul(nmr, negmu, rstd), deps=[m6])
            sc.op("dve", lambda v_: v_.drain())
            deps8 = [m7, src_dep]
            m8 = sc.eop("act", lambda a: a.activation(out_ap, src_ap, AF.Identity,
                                                      bias=nmr, scale=rstd),
                        deps=deps8)
            out = {"out": m8, "rstd": rstd, "rstd_m": m6,
                   "lnv": lnv, "lnv_m": m5}
            if need_mu_sd:
                out.update({"mu": mu, "mu_m": m_mu, "sd": sd, "sd_m": msd})
            return out

        ROT = PS_ROT

        for c in range(NCH):
            for g in range(2):
                g4 = g * 4
                xin_slice = x_d[(c * 8 + g4) * 128:(c * 8 + g4 + 4) * 128, :]
                dst = resid[:, g4 * 1024:(g4 + 4) * 1024].rearrange(
                    "p (tl d) -> p tl d", tl=4)
                src = xin_slice.rearrange("(tl p) d -> p tl d", p=128)
                deps = []
                for tl in range(4):
                    deps += rwrite(Rtile[g4 + tl], None)
                m_x = sc.op("sp", lambda s, dst=dst, src=src: s.dma_start(out=dst, in_=src),
                            deps=deps, inc="DS", n=16)
                for tl in range(4):
                    Rtile[g4 + tl].w = m_x

                lx1_done = []
                for tl in range(4):
                    tc = g4 + tl
                    rt = resid[:, tc * 1024:(tc + 1) * 1024]
                    base = (tc % 2) * 48 + 25
                    r = emit_ln(rt, Rtile[tc].w, base, lx1tm[:, :], lx1tm[:, :],
                                extra_out_deps=rwrite(R["lx1tm"], None))
                    R["lx1tm"].w = r["out"]
                    Rtile[tc].r.append(r["out"])
                    for hv in range(2):
                        def tr4(t, hv=hv):
                            ins = None
                            for k in range(4):
                                kk = hv * 4 + k
                                ins = t.transpose(
                                    psA[:, PS_T7 + k * 128:PS_T7 + (k + 1) * 128],
                                    lx1tm[:, kk * 128:(kk + 1) * 128], identf[:, :])
                            return ins
                        mt = sc.eop("pe", tr4, deps=[rread(R["lx1tm"], None),
                                                     R["static"].w] + ps_wdeps(PS_T7, 512))
                        R["lx1tm"].r.append(mt)
                        ps_setw(PS_T7, 512, mt)
                        dstap = arena[:, A_LX1:A_LX1 + 4096].rearrange(
                            "p (dc t) -> p dc t", dc=8)[:, hv * 4:(hv + 1) * 4,
                                                        tl * 128:(tl + 1) * 128]
                        srcap = psA[:, PS_T7:PS_T7 + 512].rearrange(
                            "p (k t) -> p k t", k=4)
                        wdep = rwrite(R["Lx1fm"], None) if (tl == 0 and hv == 0) else []
                        mc = sc.eop("dve", lambda v, d=dstap, s=srcap: v.tensor_copy(d, s),
                                    deps=ps_rdeps(PS_T7, 512) + wdep)
                        ps_setr(PS_T7, 512, mc)
                        lx1_done.append(mc)
                R["Lx1fm"].w = lx1_done[-1]
                for mm_ in lx1_done:
                    R["Lx1fm"].r.append(mm_)
                R["Lx1fm"].r = []
                lx1_all = list(lx1_done)

                kq_loads = []
                for kc in range(16):
                    slot = kc % 2
                    sl = A_KQP + slot * 1024
                    dd = rwrite(R[f"kqp{slot}"], None) + [R["static"].w]
                    if g == 0 and kc == 0:
                        dd += rpeek(R["hsb"])
                    mld = sc.op("pool", lambda g_, kc=kc, sl=sl: g_.dma_start(
                        out=arena[:, sl:sl + 1024], in_=wkqv_d[kc]),
                        deps=dd, inc="DG", n=16)
                    R[f"kqp{slot}"].w = mld
                    kq_loads.append(mld)
                    rot = ROT[kc % 2]
                    def kq_mms(t, sl=sl, rot=rot):
                        ins = None
                        for dc in range(8):
                            ins = t.matmul(
                                psA[:, rot:rot + 512],
                                arena[:, sl + dc * 128:sl + (dc + 1) * 128],
                                arena[:, A_LX1 + dc * 512:A_LX1 + (dc + 1) * 512],
                                start=(dc == 0), stop=(dc == 7))
                        return ins
                    mmk = sc.eop("pe", kq_mms,
                                 deps=[mld, R["Lx1fm"].w] + ps_wdeps(rot, 512))
                    R[f"kqp{slot}"].r.append(mmk)
                    R["Lx1fm"].r.append(mmk)
                    ps_setw(rot, 512, mmk)
                    if kc < 8:
                        dstap = arena[:, A_K + kc * 1024 + g4 * 128:
                                      A_K + kc * 1024 + g4 * 128 + 512]
                        wd = rwrite(R["Kbuf"], None) if (kc == 0 and g == 0) else []
                        mev = sc.eop("dve", lambda v, d=dstap, rot=rot: v.tensor_copy(
                            d, psA[:, rot:rot + 512]), deps=ps_rdeps(rot, 512) + wd)
                    else:
                        qc = kc - 8
                        dstap = arena[:, A_Q + qc * 512:A_Q + qc * 512 + 512]
                        wd = rwrite(R["Qbuf"], None) if kc == 8 else []
                        mev = sc.eop("dve", lambda v, d=dstap, rot=rot: v.tensor_scalar_mul(
                            d, psA[:, rot:rot + 512], SCALE),
                            deps=ps_rdeps(rot, 512) + wd)
                    ps_setr(rot, 512, mev)
                    if kc < 8:
                        R["Kbuf"].w = mev
                    else:
                        R["Qbuf"].w = mev
                vp_deps = rwrite(R["VP"], None) + [R["static"].w]
                vload = []
                for vp in range(8):
                    mld = sc.op("pool", lambda g_, vp=vp: g_.dma_start(
                        out=arena[:, A_VP + vp * 1024:A_VP + (vp + 1) * 1024],
                        in_=wkqv_d[16 + vp]),
                        deps=vp_deps if vp == 0 else [], inc="DG", n=16)
                    vload.append(mld)
                R["VP"].w = vload[-1]
                for tl in range(4):
                    tc = g4 + tl
                    for vh in range(2):
                        rot = ROT[(tl * 2 + vh) % 2]
                        def v_mms(t, tl=tl, vh=vh, rot=rot):
                            ins = None
                            for n_ in range(4):
                                for dc in range(8):
                                    ins = t.matmul(
                                        psA[:, rot + n_ * 128:rot + (n_ + 1) * 128],
                                        arena[:, A_LX1 + dc * 512 + tl * 128:
                                              A_LX1 + dc * 512 + (tl + 1) * 128],
                                        arena[:, A_VP + (vh * 4 + n_) * 1024 + dc * 128:
                                              A_VP + (vh * 4 + n_) * 1024 + (dc + 1) * 128],
                                        start=(dc == 0), stop=(dc == 7))
                            return ins
                        mmv = sc.eop("pe", v_mms,
                                     deps=[vload[vh * 4 + 3], R["Lx1fm"].w]
                                     + ps_wdeps(rot, 512))
                        R["VP"].r.append(mmv)
                        R["Lx1fm"].r.append(mmv)
                        ps_setw(rot, 512, mmv)
                        dstap = arena[:, A_V + tc * 1024 + vh * 512:
                                      A_V + tc * 1024 + (vh + 1) * 512]
                        wd = rwrite(R["Vbuf"], None) if (g == 0 and tl == 0 and vh == 0) else []
                        mev = sc.eop("dve", lambda v, d=dstap, rot=rot: v.tensor_copy(
                            d, psA[:, rot:rot + 512]), deps=ps_rdeps(rot, 512) + wd)
                        ps_setr(rot, 512, mev)
                        R["Vbuf"].w = mev

                if g == 1:
                    kd = rwrite(R["ktail"], None)
                    mkt = sc.eop("dve", lambda v: v.tensor_copy(
                        ktail[:, :].rearrange("p (fc t) -> p fc t", fc=8),
                        arena[:, A_K:A_K + 8192].rearrange(
                            "p (fc t) -> p fc t", fc=8)[:, :, 512:1024]),
                        deps=kd + [R["Kbuf"].w])
                    R["ktail"].w = mkt
                    R["Kbuf"].r.append(mkt)
                    vd = rwrite(R["vtail"], None)
                    mvt = sc.eop("dve", lambda v: v.tensor_copy(
                        vtail[:, :], arena[:, A_V + 4096:A_V + 8192]),
                        deps=vd + [R["Vbuf"].w])
                    R["vtail"].w = mvt
                    R["Vbuf"].r.append(mvt)

                for tl in range(4):
                    tc = g4 + tl
                    gt = c * 8 + tc
                    nw = min(gt, 4) + 1
                    ws0 = 5 - nw
                    par = tc % 2
                    base = par * 48
                    LAG = 3
                    mexp_l = [None] * H
                    for hh in range(H + LAG):
                        if hh < H:
                            h = hh
                            hp = 64 * (h % 2)
                            fcq = h // 2
                            sreg = SREG[h % 3]
                            def qk(t, tl=tl, hp=hp, fcq=fcq, nw=nw, ws0=ws0,
                                   tc=tc, sreg=sreg):
                                ins = None
                                for w in range(nw):
                                    j_c = tc - (nw - 1) + w
                                    if j_c >= 0:
                                        kap = arena[hp:hp + 64,
                                                    A_K + fcq * 1024 + j_c * 128:
                                                    A_K + fcq * 1024 + (j_c + 1) * 128]
                                    else:
                                        kap = ktail[hp:hp + 64,
                                                    fcq * 512 + (j_c + 4) * 128:
                                                    fcq * 512 + (j_c + 5) * 128]
                                    qap = arena[hp:hp + 64,
                                                A_Q + fcq * 512 + tl * 128:
                                                A_Q + fcq * 512 + (tl + 1) * 128]
                                    cs = sreg + (ws0 + w) * 128
                                    ins = t.matmul(psA[:, cs:cs + 128], kap, qap,
                                                   start=True, stop=True)
                                return ins
                            dd = [R["Qbuf"].w, R["Kbuf"].w]
                            if gt >= 4 and nw - 1 > tc:
                                dd.append(R["ktail"].w)
                            dd += ps_wdeps(sreg + ws0 * 128, nw * 128)
                            mqk = sc.eop("pe", qk, deps=dd)
                            R["Qbuf"].r.append(mqk)
                            R["Kbuf"].r.append(mqk)
                            if nw - 1 > tc:
                                R["ktail"].r.append(mqk)
                            ps_setw(sreg + ws0 * 128, nw * 128, mqk)
                            sres = R[f"sst{h % 4}"]
                            sl = ALIBI[h]
                            sap = sstage[:, (h % 4) * 640 + ws0 * 128:
                                         (h % 4) * 640 + 640]
                            wd = rwrite(sres, None)
                            mstt = sc.eop("dve", lambda v, sap=sap, sl=sl, ws0=ws0,
                                          sreg=sreg: v.scalar_tensor_tensor(
                                              sap, rel[:, ws0 * 128:640], sl,
                                              psA[:, sreg + ws0 * 128:sreg + 640],
                                              ALU.mult, ALU.add),
                                          deps=wd + ps_rdeps(sreg + ws0 * 128,
                                                             nw * 128)
                                          + [R["static"].w])
                            sres.w = mstt
                            ps_setr(sreg + ws0 * 128, nw * 128, mstt)
                            pres = R[f"ptb{h % 4}"]
                            pap = pTsb[:, (h % 4) * 640 + ws0 * 128:
                                       (h % 4) * 640 + 640]
                            wd = rwrite(pres, None)
                            mexp = sc.eop("act", lambda a, pap=pap, sap=sap:
                                          a.activation(pap, sap, AF.Exp),
                                          deps=[mstt] + wd)
                            pres.w = mexp
                            sres.r.append(mexp)
                            mexp_l[h] = mexp
                        if hh >= LAG:
                            h = hh - LAG
                            pres = R[f"ptb{h % 4}"]
                            rcp = smx[:, h:h + 1]
                            def rsmm(t, h=h, nw=nw, ws0=ws0):
                                ins = None
                                for w in range(nw):
                                    ins = t.matmul(
                                        psA[:, PS_RS + h:PS_RS + h + 1],
                                        pTsb[:, (h % 4) * 640 + (ws0 + w) * 128:
                                             (h % 4) * 640 + (ws0 + w + 1) * 128],
                                        ones_b[:, :],
                                        start=(w == 0), stop=(w == nw - 1))
                                return ins
                            mrs = sc.eop("pe", rsmm, deps=[mexp_l[h], R["static"].w]
                                         + ps_wdeps(PS_RS + h, 1))
                            pres.r.append(mrs)
                            ps_setw(PS_RS + h, 1, mrs)
                            pvc = PS_PVC[h % 2]
                            def pv(t, h=h, nw=nw, ws0=ws0, tc=tc, pvc=pvc):
                                ins = None
                                for w in range(nw):
                                    j_c = tc - (nw - 1) + w
                                    if j_c >= 0:
                                        vap = arena[:, A_V + j_c * 1024 + h * 64:
                                                    A_V + j_c * 1024 + h * 64 + 64]
                                    else:
                                        vap = vtail[:, (j_c + 4) * 1024 + h * 64:
                                                    (j_c + 4) * 1024 + h * 64 + 64]
                                    ins = t.matmul(
                                        psA[:, pvc:pvc + 64],
                                        pTsb[:, (h % 4) * 640 + (ws0 + w) * 128:
                                             (h % 4) * 640 + (ws0 + w + 1) * 128],
                                        vap, start=(w == 0), stop=(w == nw - 1))
                                return ins
                            dd = [mexp_l[h], R["Vbuf"].w] + ps_wdeps(pvc, 64)
                            if nw - 1 > tc:
                                dd.append(R["vtail"].w)
                            mpv = sc.eop("pe", pv, deps=dd)
                            pres.r.append(mpv)
                            R["Vbuf"].r.append(mpv)
                            if nw - 1 > tc:
                                R["vtail"].r.append(mpv)
                            ps_setw(pvc, 64, mpv)
                            dd = [mrs] if prev_ev[h] is None else [mrs, prev_ev[h]]
                            mrc = sc.eop("dve", lambda v, rcp=rcp, h=h:
                                         v.reciprocal(rcp, psA[:, PS_RS + h:
                                                               PS_RS + h + 1]),
                                         deps=dd + ps_rdeps(PS_RS + h, 1))
                            ps_setr(PS_RS + h, 1, mrc)
                            ares = R["attnsb"]
                            wd = rwrite(ares, None) if h == 0 else []
                            mev = sc.eop("act", lambda a, h=h, rcp=rcp, pvc=pvc:
                                         a.activation(attnsb[:, h * 64:(h + 1) * 64],
                                                      psA[:, pvc:pvc + 64],
                                                      AF.Identity, scale=rcp),
                                         deps=[mrc] + ps_rdeps(pvc, 64) + wd)
                            ps_setr(pvc, 64, mev)
                            ares.w = mev
                            prev_ev[h] = mev

                    rt = resid[:, tc * 1024:(tc + 1) * 1024]
                    ddr = rwrite(Rtile[tc], None)
                    m_r1 = sc.eop("dve", lambda v, rt=rt: v.tensor_add(
                        rt, rt, attnsb[:, :]), deps=ddr + [R["attnsb"].w])
                    Rtile[tc].w = m_r1
                    R["attnsb"].r.append(m_r1)
                    wd = rwrite(R["lx2scr"], None)
                    r2 = emit_ln(rt, m_r1, base, lx2scr[:, :], lx2scr[:, :],
                                 extra_out_deps=wd)
                    R["lx2scr"].w = r2["out"]
                    Rtile[tc].r.append(r2["out"])
                    ddr = rwrite(Rtile[tc], None)
                    m_r15 = sc.eop("dve", lambda v, rt=rt: v.tensor_add(
                        rt, rt, lx2scr[:, :]), deps=ddr + [r2["out"]])
                    Rtile[tc].w = m_r15
                    R["lx2scr"].r.append(m_r15)
                    e3 = stats[:, base + 34:base + 35]
                    t3 = stats[:, base + 35:base + 36]
                    l3 = stats[:, base + 36:base + 37]
                    rstd3 = stats[:, base + 37:base + 38]
                    sd3 = stats[:, base + 38:base + 39]
                    m_e3 = sc.eop("act", lambda a, e3=e3, lv=r2["lnv"]: a.activation(
                        e3, lv, AF.Exp, scale=-1.0), deps=[r2["lnv_m"]])
                    m_t3 = sc.eop("dve", lambda v, t3=t3, e3=e3: v.tensor_scalar(
                        t3, e3, -EPS, 1.0 + EPS, ALU.mult, ALU.add), deps=[m_e3])
                    m_l3 = sc.eop("act", lambda a, l3=l3, t3=t3: a.activation(
                        l3, t3, AF.Ln), deps=[m_t3])
                    m_r3 = sc.eop("act", lambda a, rstd3=rstd3, l3=l3: a.activation(
                        rstd3, l3, AF.Exp, scale=-0.5))
                    m_s3 = sc.eop("act", lambda a, sd3=sd3, l3=l3: a.activation(
                        sd3, l3, AF.Exp, scale=0.5))
                    wd = rwrite(R["xn3tm"], None)
                    m_x3 = sc.eop("dve", lambda v, rstd3=rstd3: v.tensor_scalar_mul(
                        xn3tm[:, :], lx2scr[:, :], rstd3),
                        deps=[m_r3, R["lx2scr"].w] + wd)
                    R["xn3tm"].w = m_x3
                    R["lx2scr"].r.append(m_x3)
                    cps = []
                    for hv in range(2):
                        def tr4(t, hv=hv):
                            ins = None
                            for k in range(4):
                                kk = hv * 4 + k
                                ins = t.transpose(
                                    psA[:, PS_T7 + k * 128:PS_T7 + (k + 1) * 128],
                                    xn3tm[:, kk * 128:(kk + 1) * 128], identf[:, :])
                            return ins
                        mt = sc.eop("pe", tr4, deps=[R["xn3tm"].w]
                                    + ps_wdeps(PS_T7, 512))
                        R["xn3tm"].r.append(mt)
                        ps_setw(PS_T7, 512, mt)
                        srcap = psA[:, PS_T7:PS_T7 + 512].rearrange(
                            "p (k t) -> p k t", k=4)
                        dst1 = xn3fm[:, :].rearrange("p (dc t) -> p dc t", dc=8)[
                            :, hv * 4:(hv + 1) * 4, tc * 128:(tc + 1) * 128]
                        wd = rwrite(R["xn3fm"], None) if (tc == 0 and hv == 0) else []
                        mc1 = sc.eop("dve", lambda v, d=dst1, s=srcap: v.tensor_copy(d, s),
                                     deps=ps_rdeps(PS_T7, 512) + wd)
                        R["xn3fm"].w = mc1
                        wd = rwrite(R["xn3f32"], None) if hv == 0 else []
                        dst2 = xn3f32[:, hv * 512:(hv + 1) * 512]
                        mc2 = sc.eop("dve", lambda v, d=dst2, s=psA[:, PS_T7:PS_T7 + 512]:
                                     v.tensor_copy(d, s),
                                     deps=ps_rdeps(PS_T7, 512) + wd)
                        R["xn3f32"].w = mc2
                        ps_setr(PS_T7, 512, mc1)
                        ps_setr(PS_T7, 512, mc2)
                        cps.append(mc2)
                    def aff_mm(t):
                        ins = None
                        for dc in range(8):
                            ins = t.matmul(psA[:, PS_AFF:PS_AFF + 8],
                                           xn3f32[:, dc * 128:(dc + 1) * 128],
                                           centc[:, dc * 8:(dc + 1) * 8],
                                           start=(dc == 0), stop=(dc == 7))
                        return ins
                    maff = sc.eop("pe", aff_mm, deps=[R["xn3f32"].w, R["centc"].w]
                                  + ps_wdeps(PS_AFF, 8))
                    R["xn3f32"].r.append(maff)
                    ps_setw(PS_AFF, 8, maff)
                    affs = stats[:, base + 17:base + 25]
                    m_t = sc.eop("dve", lambda v, affs=affs, sdap=sd3:
                                 v.tensor_scalar_mul(
                                     affs, psA[:, PS_AFF:PS_AFF + 8], sdap),
                                 deps=[m_s3] + ps_rdeps(PS_AFF, 8))
                    ps_setr(PS_AFF, 8, m_t)
                    sc.op("dve", lambda v: v.drain())
                    mxa = stc(base, "maxa")
                    m_mx = sc.eop("dve", lambda v, affs=affs, mxa=mxa: v.reduce_max(
                        mxa, affs, axis=AX.X))
                    sc.op("dve", lambda v: v.drain())
                    ohap = onehot[:, tc * 8:(tc + 1) * 8]
                    wd = rwrite(R["onehot"], None) if tc == 0 else []
                    m_oh = sc.eop("dve", lambda v, ohap=ohap, affs=affs, mxa=mxa:
                                  v.tensor_scalar(ohap, affs, mxa, None, ALU.is_equal),
                                  deps=wd)
                    R["onehot"].w = m_oh
                    eal = stc(base, "ealpha")
                    m_e = sc.eop("act", lambda a, eal=eal, mxa=mxa: a.activation(
                        eal, mxa, AF.Exp, scale=-1.0), deps=[m_mx])
                    tal = stc(base, "talpha")
                    m_t2 = sc.eop("dve", lambda v, tal=tal, eal=eal: v.tensor_scalar_add(
                        tal, eal, 1.0), deps=[m_e])
                    sc.op("dve", lambda v: v.drain())
                    alap = alpha[:, tc:tc + 1]
                    wd = rwrite(R["alpha"], None) if tc == 0 else []
                    m_al = sc.eop("dve", lambda v, alap=alap, tal=tal: v.reciprocal(
                        alap, tal), deps=wd)
                    R["alpha"].w = m_al
                    sc.op("dve", lambda v: v.drain())
                    amap = amask[:, tc * 8:(tc + 1) * 8]
                    wd = rwrite(R["amask"], None) if tc == 0 else []
                    m_am = sc.eop("dve", lambda v, amap=amap, ohap=ohap, alap=alap:
                                  v.tensor_scalar_mul(amap, ohap, alap),
                                  deps=wd + [m_oh])
                    R["amask"].w = m_am
                    m_ot = sc.eop("pe", lambda t, ohap=ohap: t.transpose(
                        psA[0:8, PS_ONET:PS_ONET + 128], ohap, identf[:, :]),
                        deps=[m_oh] + ps_wdeps(PS_ONET, 128))
                    R["onehot"].r.append(m_ot)
                    ps_setw(PS_ONET, 128, m_ot)
                    otap = oneT[0:8, tc * 128:(tc + 1) * 128]
                    wd = rwrite(R["oneT"], None) if tc == 0 else []
                    m_otc = sc.eop("dve", lambda v, otap=otap: v.tensor_copy(
                        otap, psA[0:8, PS_ONET:PS_ONET + 128]),
                        deps=ps_rdeps(PS_ONET, 128) + wd)
                    R["oneT"].w = m_otc
                    ps_setr(PS_ONET, 128, m_otc)

            for e in range(NE):
                for th in range(2):
                    ti = (e * 2 + th)
                    slot = ti % 2
                    xres = R[f"xe{slot}"]
                    wd = rwrite(xres, None)
                    xe0 = A_XE + slot * 4096
                    m_xe = None
                    for dc in range(8):
                        src = xn3fm[:, dc * 1024 + th * 512:dc * 1024 + (th + 1) * 512]
                        dst = arena[:, xe0 + dc * 512:xe0 + (dc + 1) * 512]
                        gap = gc[:, dc * 8 + e:dc * 8 + e + 1]
                        bap = bc[:, dc * 8 + e:dc * 8 + e + 1]
                        m_xe = sc.eop("act", lambda a, dst=dst, src=src, gap=gap,
                                      bap=bap: a.activation(dst, src, AF.Identity,
                                                            bias=bap, scale=gap),
                                      deps=(wd + [R["xn3fm"].w, R["gc"].w, R["bc"].w])
                                      if dc == 0 else [])
                    xres.w = m_xe
                    R["xn3fm"].r.append(m_xe)
                    relus = []
                    for fcg in range(8):
                        wslot = fcg % 3
                        w1r = R[f"w1{wslot}"]
                        wd = rwrite(w1r, None)
                        if e == 0 and th == 0 and fcg == 0:
                            wd += rpeek(R["Vbuf"], R["Qbuf"], R["VP"])
                        w1base = A_W1 + wslot * 4096
                        src = w1_d[e, fcg]
                        dst = arena[:, w1base:w1base + 4096]
                        mld = sc.op("pool", lambda g_, dst=dst, src=src: g_.dma_start(
                            out=dst, in_=src), deps=wd, inc="DG", n=16)
                        w1r.w = mld
                        for fc4 in range(4):
                            fc = fcg * 4 + fc4
                            rot = ROT[fc % 2]
                            def hmm(t, w1base=w1base, fc4=fc4, xe0=xe0, rot=rot):
                                ins = None
                                for dc in range(8):
                                    ins = t.matmul(
                                        psA[:, rot:rot + 512],
                                        arena[:, w1base + fc4 * 1024 + dc * 128:
                                              w1base + fc4 * 1024 + (dc + 1) * 128],
                                        arena[:, xe0 + dc * 512:xe0 + (dc + 1) * 512],
                                        start=(dc == 0), stop=(dc == 7))
                                return ins
                            mmh = sc.eop("pe", hmm, deps=[mld, xres.w]
                                         + ps_wdeps(rot, 512))
                            w1r.r.append(mmh)
                            xres.r.append(mmh)
                            ps_setw(rot, 512, mmh)
                            hres = R["hsb"]
                            wd2 = rwrite(hres, None) if fc == 0 else []
                            hdst = arena[:, A_H + fc * 512:A_H + (fc + 1) * 512]
                            b1ap = b1c[:, fc * 8 + e:fc * 8 + e + 1]
                            mrl = sc.eop("act", lambda a, hdst=hdst, b1ap=b1ap, rot=rot:
                                         a.activation(hdst, psA[:, rot:rot + 512],
                                                      AF.Relu, bias=b1ap),
                                         deps=ps_rdeps(rot, 512) + wd2 + [R["b1c"].w])
                            ps_setr(rot, 512, mrl)
                            hres.w = mrl
                            relus.append(mrl)
                    evacs = {}
                    for fcg in range(8):
                        wslot = fcg % 3
                        w2r = R[f"w2{wslot}"]
                        wd = rwrite(w2r, None)
                        w2base = A_W2 + wslot * 4096
                        src = w2_d[e, fcg]
                        dst = arena[:, w2base:w2base + 4096]
                        mld = sc.op("pool", lambda g_, dst=dst, src=src: g_.dma_start(
                            out=dst, in_=src), deps=wd, inc="DG", n=16)
                        w2r.w = mld
                        for sub in range(4):
                            for dh in range(2):
                                co = PSO[(sub, dh)]
                                def omm(t, w2base=w2base, fcg=fcg, sub=sub, dh=dh, co=co):
                                    ins = None
                                    for fc4 in range(4):
                                        fc = fcg * 4 + fc4
                                        ins = t.matmul(
                                            psA[:, co:co + 512],
                                            arena[:, A_H + fc * 512 + sub * 128:
                                                  A_H + fc * 512 + (sub + 1) * 128],
                                            arena[:, w2base + fc4 * 1024 + dh * 512:
                                                  w2base + fc4 * 1024 + (dh + 1) * 512],
                                            start=(fcg == 0 and fc4 == 0),
                                            stop=(fcg == 7 and fc4 == 3))
                                    return ins
                                dd = [mld, relus[fcg * 4 + 3]]
                                if fcg == 0:
                                    dd += ps_wdeps(co, 512)
                                mo = sc.eop("pe", omm, deps=dd)
                                w2r.r.append(mo)
                                R["hsb"].r.append(mo)
                                if fcg == 0:
                                    ps_setw(co, 512, mo)
                                else:
                                    for b in ps_banks(co, 512):
                                        bank[b].w = mo
                                if fcg == 7:
                                    evacs[(sub, dh)] = mo
                    for sub in (3, 0, 1, 2):
                        tile_c = th * 4 + sub
                        amcol = amask[:, tile_c * 8 + e:tile_c * 8 + e + 1]
                        for dh in range(2):
                            co = PSO[(sub, dh)]
                            adst = resid[:, tile_c * 1024 + dh * 512:
                                         tile_c * 1024 + (dh + 1) * 512]
                            mev = sc.eop("dve", lambda v, adst=adst, amcol=amcol,
                                         co=co: v.scalar_tensor_tensor(
                                             adst, psA[:, co:co + 512], amcol,
                                             adst, ALU.mult, ALU.add),
                                         deps=[evacs[(sub, dh)], R["amask"].w,
                                               Rtile[tile_c].w])
                            R["amask"].r.append(mev)
                            ps_setr(co, 512, mev)
                            Rtile[tile_c].w = mev

            for tc in range(8):
                otap = oneT[0:8, tc * 128:(tc + 1) * 128]
                mmb = []
                for dh in range(2):
                    co = PS_B2[dh]
                    m_ = sc.eop("pe", lambda t, otap=otap, dh=dh, co=co: t.matmul(
                        psA[:, co:co + 512], otap, b2sb[0:8, dh * 512:(dh + 1) * 512],
                        start=True, stop=True),
                        deps=[R["oneT"].w, R["b2sb"].w] + ps_wdeps(co, 512))
                    R["oneT"].r.append(m_)
                    ps_setw(co, 512, m_)
                    mmb.append(m_)
                rt = resid[:, tc * 1024:(tc + 1) * 1024]
                alap = alpha[:, tc:tc + 1]
                souts = []
                for dh in range(2):
                    co = PS_B2[dh]
                    m_ = sc.eop("dve", lambda v, dh=dh, co=co, rt=rt, alap=alap:
                                v.scalar_tensor_tensor(
                                    rt[:, dh * 512:(dh + 1) * 512],
                                    psA[:, co:co + 512], alap,
                                    rt[:, dh * 512:(dh + 1) * 512],
                                    ALU.mult, ALU.add),
                                deps=[mmb[dh], R["alpha"].w, Rtile[tc].w])
                    ps_setr(co, 512, m_)
                    R["alpha"].r.append(m_)
                    souts.append(m_)
                Rtile[tc].w = souts[-1]
                ydst = y_d[(c * 8 + tc) * 128:(c * 8 + tc + 1) * 128, :]
                m_o = sc.op("sp", lambda s, ydst=ydst, rt=rt: s.dma_start(
                    out=ydst, in_=rt), deps=souts, inc="DO", n=16)
                Rtile[tc].r.append(m_o)

        total_out = sc.C["DO"]
        sc.op("sp", lambda s: s.nop(), deps=[("DO", total_out)])

        def play(e, name):
            for deps, fn, inc, n in sc.q[name]:
                for s_, v_ in deps:
                    e.wait_ge(sems[s_], v_)
                ins = fn(e)
                if inc is not None and ins is not None:
                    ins.then_inc(sems[inc], n)

        @block.tensor
        def _(t):
            play(t, "pe")

        @block.scalar
        def _(a):
            play(a, "act")

        @block.vector
        def _(v):
            play(v, "dve")

        @block.gpsimd
        def _(g):
            play(g, "pool")

        @block.sync
        def _(s):
            play(s, "sp")

    return nc


def kernel(L0_attn_x, L0_attn_w_kqv, L0_ffn_centroids, L0_ffn_ln_g, L0_ffn_ln_b,
           L0_ffn_w1, L0_ffn_b1, L0_ffn_w2, L0_ffn_b2):
    x = np.ascontiguousarray(np.asarray(L0_attn_x, dtype=np.float32))
    wkqv = np.asarray(L0_attn_w_kqv, dtype=np.float32)
    cent = np.asarray(L0_ffn_centroids, dtype=np.float32)
    ln_g = np.asarray(L0_ffn_ln_g, dtype=np.float32)
    ln_b = np.asarray(L0_ffn_ln_b, dtype=np.float32)
    w1 = np.asarray(L0_ffn_w1, dtype=np.float32)
    b1 = np.asarray(L0_ffn_b1, dtype=np.float32)
    w2 = np.asarray(L0_ffn_w2, dtype=np.float32)
    b2 = np.asarray(L0_ffn_b2, dtype=np.float32)

    wkqv_p = np.ascontiguousarray(
        wkqv.reshape(8, 128, 24, 128).transpose(2, 1, 0, 3))
    wkqv_p = wkqv_p.reshape(24, 128, 1024)
    w1_p = np.ascontiguousarray(
        w1.reshape(NE, 8, 128, 32, 128).transpose(0, 3, 2, 1, 4)
        .reshape(NE, 8, 4, 128, 1024).transpose(0, 1, 3, 2, 4)).reshape(NE, 8, 128, 4096)
    w2_p = np.ascontiguousarray(
        w2.reshape(NE, 8, 4, 128, 1024).transpose(0, 1, 3, 2, 4)).reshape(NE, 8, 128, 4096)
    b1c = np.ascontiguousarray(b1.T.reshape(32, 128, NE).transpose(1, 0, 2)).reshape(128, 256)
    gcq = np.ascontiguousarray(ln_g.T.reshape(8, 128, NE).transpose(1, 0, 2)).reshape(128, 64)
    bcq = np.ascontiguousarray(ln_b.T.reshape(8, 128, NE).transpose(1, 0, 2)).reshape(128, 64)
    centc = np.ascontiguousarray(cent.T.reshape(8, 128, NE).transpose(1, 0, 2)).reshape(128, 64)

    nc = build_program()
    in_maps = []
    for b in range(NCORES):
        in_maps.append({
            "x": np.ascontiguousarray(x[b]),
            "wkqv": wkqv_p, "w1": w1_p, "w2": w2_p,
            "b1c": b1c, "gc": gcq, "bc": bcq, "centc": centc, "b2": b2,
        })
    res = run_bass_kernel_spmd(nc, in_maps, core_ids=list(range(NCORES)))
    out = np.stack([np.asarray(res.results[b]["y"]) for b in range(NCORES)], axis=0)
    return out.astype(np.float32)


# version 18 (final)
# speedup vs baseline: 1.0818x; 1.0818x over previous
import math
import contextlib
import numpy as np

import concourse.bass as bass
import concourse.mybir as mybir
from concourse.bass_utils import run_bass_kernel_spmd

F32 = mybir.dt.float32
BF16 = mybir.dt.bfloat16
AF = mybir.ActivationFunctionType
ALU = mybir.AluOpType
AX = mybir.AxisListType

NCORES = 8
S = 4096
D = 1024
H = 16
HD = 64
WIN = 512
FF = 4096
NE = 8
EPS = 1e-5
TILES = 32
NCH = 4
SCALE = 1.0 / 32.0

ALIBI = [1.0 / ((2.0 ** 8) ** (1.0 / 16)) ** (i + 1) for i in range(H)]

A_K = 0
A_V = 8192
A_Q = 16384
A_LX1 = 20480
A_KQP = 24576
A_VP = 26624
A_PT = 34816
A_XE = 0
A_W1 = 8192
A_H = 20480
A_W2 = 36864
ARENA_COLS = 53248

PS_ROT = (0, 512)
SREG = (0, 1024, 2048)
PS_RS = 3072
PS_PVC = (3104, 3168)
PS_AFF = 3232
PS_ONET = 3264
PS_T7 = 3584
PSO = {(0, 0): 1024, (0, 1): 1536, (1, 0): 2048, (1, 1): 2560,
       (2, 0): 3072, (2, 1): 3584, (3, 0): 0, (3, 1): 512}
PS_B2 = (1024, 1536)

ENGS = ("pe", "act", "dve", "pool", "sp")
SEMS = ("P", "A", "V", "G", "DG", "DS", "DO")
ESEM = {"pe": "P", "act": "A", "dve": "V", "pool": "G", "sp": None}


class Res:
    __slots__ = ("w", "r")

    def __init__(self, w=None):
        self.w = w
        self.r = []


class Sched:
    def __init__(self):
        self.q = {e: [] for e in ENGS}
        self.C = {s: 0 for s in SEMS}
        self.lastw = {e: {s: 0 for s in SEMS} for e in ENGS}

    def op(self, eng, fn, deps=(), inc=None, n=1):
        dd = []
        for d in deps:
            if d is None:
                continue
            s, v = d
            if v > self.lastw[eng][s]:
                dd.append((s, v))
                self.lastw[eng][s] = v
        mile = None
        if inc is not None:
            self.C[inc] += n
            mile = (inc, self.C[inc])
        self.q[eng].append((dd, fn, inc, n))
        return mile

    def eop(self, eng, fn, deps=()):
        return self.op(eng, fn, deps, inc=ESEM[eng], n=1)


def rpeek(*ress):
    best = {}
    for res in ress:
        for m in list(res.r) + ([res.w] if res.w is not None else []):
            s, v = m
            if v > best.get(s, -1):
                best[s] = v
    return list(best.items())


def rread(res, mile):
    if mile is not None:
        res.r.append(mile)
    return res.w


def rwrite(res, mile):
    deps = list(res.r)
    if res.w is not None:
        deps.append(res.w)
    res.r = []
    res.w = mile
    return deps


def build_program():
    nc = bass.Bass("TRN2", target_bir_lowering=False)
    x_d = nc.dram_tensor("x", [S, D], F32, kind="ExternalInput").ap()
    wkqv_d = nc.dram_tensor("wkqv", [24, 128, 1024], F32, kind="ExternalInput").ap()
    w1_d = nc.dram_tensor("w1", [NE, 8, 128, 4096], F32, kind="ExternalInput").ap()
    w2_d = nc.dram_tensor("w2", [NE, 8, 128, 4096], F32, kind="ExternalInput").ap()
    b1c_d = nc.dram_tensor("b1c", [128, 256], F32, kind="ExternalInput").ap()
    gc_d = nc.dram_tensor("gc", [128, 64], F32, kind="ExternalInput").ap()
    bc_d = nc.dram_tensor("bc", [128, 64], F32, kind="ExternalInput").ap()
    centc_d = nc.dram_tensor("centc", [128, 64], F32, kind="ExternalInput").ap()
    b2_d = nc.dram_tensor("b2", [NE, D], F32, kind="ExternalInput").ap()
    y_d = nc.dram_tensor("y", [S, D], F32, kind="ExternalOutput").ap()

    with contextlib.ExitStack() as es:
        E = es.enter_context
        arena = E(nc.sbuf_tensor("arena", [128, ARENA_COLS], BF16))
        resid = E(nc.sbuf_tensor("resid", [128, 8 * 1024], F32))
        xn3fm = E(nc.sbuf_tensor("xn3fm", [128, 8 * 1024], BF16))
        ktail = E(nc.sbuf_tensor("ktail", [128, 8 * 512], BF16))
        vtail = E(nc.sbuf_tensor("vtail", [128, 4 * 1024], BF16))
        lx1tm = E(nc.sbuf_tensor("lx1tm", [128, 1024], F32))
        lx2scr = E(nc.sbuf_tensor("lx2scr", [128, 1024], F32))
        xn3tm = E(nc.sbuf_tensor("xn3tm", [128, 1024], F32))
        xn3f32 = E(nc.sbuf_tensor("xn3f32", [128, 1024], F32))
        sstage = E(nc.sbuf_tensor("sstage", [128, 2 * 640], BF16))
        pTsb = E(nc.sbuf_tensor("pTsb", [128, 4 * 640], BF16))
        ones_b = E(nc.sbuf_tensor("ones_b", [128, 1], BF16))
        attnsb = E(nc.sbuf_tensor("attnsb", [128, 1024], F32))
        smx = E(nc.sbuf_tensor("smx", [128, 16], F32))
        amask = E(nc.sbuf_tensor("amask", [128, 64], F32))
        rel = E(nc.sbuf_tensor("rel", [128, 640], F32))
        identf = E(nc.sbuf_tensor("identf", [128, 128], F32))
        b1c = E(nc.sbuf_tensor("b1c_s", [128, 256], F32))
        gc = E(nc.sbuf_tensor("gc_s", [128, 64], F32))
        bc = E(nc.sbuf_tensor("bc_s", [128, 64], F32))
        centc = E(nc.sbuf_tensor("centc_s", [128, 64], F32))
        b2sb = E(nc.sbuf_tensor("b2sb", [NE, D], BF16))
        onehot = E(nc.sbuf_tensor("onehot", [128, 64], F32))
        oneT = E(nc.sbuf_tensor("oneT", [NE, 8 * 128], BF16))
        alpha = E(nc.sbuf_tensor("alpha", [128, 8], F32))
        stats = E(nc.sbuf_tensor("stats", [128, 96], F32))
        ones_c = E(nc.sbuf_tensor("ones_c", [128, 1], F32))
        ones_r = E(nc.sbuf_tensor("ones_r", [1, 128], F32))
        csr = E(nc.sbuf_tensor("csr", [8, 2], F32))
        csrow = E(nc.sbuf_tensor("csrow", [1, 8], F32))
        csumb = E(nc.sbuf_tensor("csumb", [128, 8], F32))
        psA = E(nc.psum_tensor("psA", [128, 4096], F32))
        sems = {s: E(nc.semaphore(s)) for s in SEMS}
        block = E(nc.Block())

        sc = Sched()
        bank = [Res() for _ in range(8)]

        def ps_w(mile_placeholder_cols):
            pass

        def ps_banks(c0, n):
            b0 = c0 // 512
            b1 = (c0 + n - 1) // 512
            return range(b0, b1 + 1)

        def ps_wdeps(c0, n):
            dd = []
            for b in ps_banks(c0, n):
                dd.extend(bank[b].r)
                if bank[b].w is not None:
                    dd.append(bank[b].w)
                bank[b].r = []
            return dd

        def ps_setw(c0, n, mile):
            for b in ps_banks(c0, n):
                bank[b].w = mile

        def ps_rdeps(c0, n):
            return [bank[b].w for b in ps_banks(c0, n) if bank[b].w is not None]

        def ps_setr(c0, n, mile):
            if mile is None:
                return
            for b in ps_banks(c0, n):
                bank[b].r.append(mile)

        R = {name: Res() for name in (
            "Kbuf", "Vbuf", "Qbuf", "Lx1fm", "VP", "ktail", "vtail",
            "xn3fm", "onehot", "oneT", "alpha",
            "lx1tm", "lx2scr", "xn3tm", "xn3f32", "hsb",
        )}
        for nm in (["kqp0", "kqp1", "xe0", "xe1", "w10", "w11", "w12", "w13", "w20", "w21", "w22", "w23",
                    "attnsb", "amask"]
                   + [f"sst{i}" for i in range(2)]
                   + [f"ptb{i}" for i in range(4)]):
            R[nm] = Res()
        prev_ev = [None] * H
        Rtile = [Res() for _ in range(8)]

        st_i = {"negs": 0, "negmu": 1, "v": 2, "ssq": 3, "nmr": 4, "mu": 5,
                "maxa": 6, "ealpha": 7, "talpha": 8}

        def stc(base, name):
            i = base + st_i[name]
            return stats[:, i:i + 1]

        m = sc.op("pool", lambda g: g.dma_start(out=b1c[:, :], in_=b1c_d), inc="DG", n=16)
        R["b1c"] = Res(m)
        m = sc.op("pool", lambda g: g.dma_start(out=gc[:, :], in_=gc_d), inc="DG", n=16)
        R["gc"] = Res(m)
        m = sc.op("pool", lambda g: g.dma_start(out=bc[:, :], in_=bc_d), inc="DG", n=16)
        R["bc"] = Res(m)
        m = sc.op("pool", lambda g: g.dma_start(out=centc[:, :], in_=centc_d), inc="DG", n=16)
        R["centc"] = Res(m)
        m = sc.op("pool", lambda g: g.dma_start(out=b2sb[:, :], in_=b2_d), inc="DG", n=16)
        R["b2sb"] = Res(m)

        def startup_pool(g):
            for w in range(5):
                cb = (w - 4) * 128
                g.iota(rel[:, w * 128:(w + 1) * 128], [[-1, 128]], base=cb,
                       channel_multiplier=1,
                       allow_small_or_imprecise_dtypes=True)
                g.drain()
                g.affine_select(out=rel[:, w * 128:(w + 1) * 128],
                                in_=rel[:, w * 128:(w + 1) * 128],
                                compare_op=ALU.is_ge, fill=-1.0e30, base=-cb,
                                pattern=[[1, 128]], channel_multiplier=-1)
                g.drain()
                g.affine_select(out=rel[:, w * 128:(w + 1) * 128],
                                in_=rel[:, w * 128:(w + 1) * 128],
                                compare_op=ALU.is_ge, fill=-1.0e30, base=cb + 512,
                                pattern=[[-1, 128]], channel_multiplier=1)
                g.drain()
            g.memset(ones_b[:, :], 1.0)
            g.memset(identf[:, :], 0.0)
            g.drain()
            g.affine_select(out=identf[:, :], in_=identf[:, :], compare_op=ALU.not_equal,
                            fill=1.0, base=0, pattern=[[-1, 128]], channel_multiplier=1)
            g.memset(ones_c[:, :], 1.0)
            return g.memset(ones_r[:, :], 1.0)
        m_start = sc.eop("pool", startup_pool)
        R["static"] = Res(m_start)

        def csum_mms(t):
            ins = None
            for dc in range(8):
                ins = t.matmul(psA[0:8, 1024:1025], centc[:, dc * 8:(dc + 1) * 8],
                               ones_c[:, :], start=(dc == 0), stop=(dc == 7))
            return ins
        mm = sc.eop("pe", csum_mms, deps=[R["centc"].w, m_start] + ps_wdeps(1024, 1))
        ps_setw(1024, 1, mm)
        mcp = sc.eop("dve", lambda v: v.tensor_copy(csr[0:8, 0:1], psA[0:8, 1024:1025]),
                     deps=ps_rdeps(1024, 1))
        ps_setr(1024, 1, mcp)
        mtr = sc.eop("pe", lambda t: t.transpose(psA[0:1, 1152:1160], csr[0:8, 0:1],
                                                 identf[0:8, 0:8]),
                     deps=[mcp] + ps_wdeps(1152, 8))
        ps_setw(1152, 8, mtr)
        mcr = sc.eop("dve", lambda v: v.tensor_copy(csrow[0:1, :], psA[0:1, 1152:1160]),
                     deps=ps_rdeps(1152, 8))
        ps_setr(1152, 8, mcr)
        mbc = sc.eop("pe", lambda t: t.matmul(psA[:, 1160:1168], ones_r[0:1, :],
                                              csrow[0:1, :], start=True, stop=True),
                     deps=[mcr] + ps_wdeps(1160, 8))
        ps_setw(1160, 8, mbc)
        mcb = sc.eop("dve", lambda v: v.tensor_copy(csumb[:, :], psA[:, 1160:1168]),
                     deps=ps_rdeps(1160, 8))
        ps_setr(1160, 8, mcb)
        R["csumb"] = Res(mcb)

        def emit_ln(src_ap, src_dep, base, out_ap, dump_ap, need_mu_sd=False,
                    extra_out_deps=()):
            negs, negmu, v, nmr = (stc(base, "negs"), stc(base, "negmu"),
                                   stc(base, "v"), stc(base, "nmr"))
            ssq, mu = stc(base, "ssq"), stc(base, "mu")
            m1 = sc.eop("dve", lambda v_: v_.reduce_sum(negs, src_ap, axis=AX.X,
                                                        negate=True),
                        deps=[src_dep])
            sc.op("dve", lambda v_: v_.drain())
            m2 = sc.eop("dve", lambda v_: v_.tensor_scalar_mul(negmu, negs, 1.0 / D))
            if need_mu_sd:
                m_mu = sc.eop("dve", lambda v_: v_.tensor_scalar_mul(mu, negs, -1.0 / D))
            sc.op("dve", lambda v_: v_.drain())
            m3 = sc.eop("act", lambda a: a.activation(dump_ap, src_ap, AF.Square,
                                                      bias=negmu, scale=1.0,
                                                      accum_out=ssq),
                        deps=[m2, src_dep] + list(extra_out_deps))
            m4 = sc.eop("dve", lambda v_: v_.tensor_scalar(v, ssq, 1.0 / D, EPS,
                                                           ALU.mult, ALU.add),
                        deps=[m3])
            sc.op("dve", lambda v_: v_.drain())
            lnv = stc(base, "ealpha")
            m5 = sc.eop("act", lambda a: a.activation(lnv, v, AF.Ln), deps=[m4])
            rstd = stc(base, "talpha")
            m6 = sc.eop("act", lambda a: a.activation(rstd, lnv, AF.Exp, scale=-0.5))
            msd = None
            sd = stc(base, "maxa")
            if need_mu_sd:
                msd = sc.eop("act", lambda a: a.activation(sd, lnv, AF.Exp, scale=0.5))
            m7 = sc.eop("dve", lambda v_: v_.tensor_mul(nmr, negmu, rstd), deps=[m6])
            sc.op("dve", lambda v_: v_.drain())
            deps8 = [m7, src_dep]
            m8 = sc.eop("act", lambda a: a.activation(out_ap, src_ap, AF.Identity,
                                                      bias=nmr, scale=rstd),
                        deps=deps8)
            out = {"out": m8, "rstd": rstd, "rstd_m": m6,
                   "lnv": lnv, "lnv_m": m5}
            if need_mu_sd:
                out.update({"mu": mu, "mu_m": m_mu, "sd": sd, "sd_m": msd})
            return out

        ROT = PS_ROT

        for c in range(NCH):
            for g in range(2):
                g4 = g * 4
                xin_slice = x_d[(c * 8 + g4) * 128:(c * 8 + g4 + 4) * 128, :]
                dst = resid[:, g4 * 1024:(g4 + 4) * 1024].rearrange(
                    "p (tl d) -> p tl d", tl=4)
                src = xin_slice.rearrange("(tl p) d -> p tl d", p=128)
                deps = []
                for tl in range(4):
                    deps += rwrite(Rtile[g4 + tl], None)
                m_x = sc.op("sp", lambda s, dst=dst, src=src: s.dma_start(out=dst, in_=src),
                            deps=deps, inc="DS", n=16)
                for tl in range(4):
                    Rtile[g4 + tl].w = m_x

                lx1_done = []
                for tl in range(4):
                    tc = g4 + tl
                    rt = resid[:, tc * 1024:(tc + 1) * 1024]
                    base = (tc % 2) * 48 + 25
                    r = emit_ln(rt, Rtile[tc].w, base, lx1tm[:, :], lx1tm[:, :],
                                extra_out_deps=rwrite(R["lx1tm"], None))
                    R["lx1tm"].w = r["out"]
                    Rtile[tc].r.append(r["out"])
                    for hv in range(2):
                        def tr4(t, hv=hv):
                            ins = None
                            for k in range(4):
                                kk = hv * 4 + k
                                ins = t.transpose(
                                    psA[:, PS_T7 + k * 128:PS_T7 + (k + 1) * 128],
                                    lx1tm[:, kk * 128:(kk + 1) * 128], identf[:, :])
                            return ins
                        mt = sc.eop("pe", tr4, deps=[rread(R["lx1tm"], None),
                                                     R["static"].w] + ps_wdeps(PS_T7, 512))
                        R["lx1tm"].r.append(mt)
                        ps_setw(PS_T7, 512, mt)
                        dstap = arena[:, A_LX1:A_LX1 + 4096].rearrange(
                            "p (dc t) -> p dc t", dc=8)[:, hv * 4:(hv + 1) * 4,
                                                        tl * 128:(tl + 1) * 128]
                        srcap = psA[:, PS_T7:PS_T7 + 512].rearrange(
                            "p (k t) -> p k t", k=4)
                        wdep = rwrite(R["Lx1fm"], None) if (tl == 0 and hv == 0) else []
                        mc = sc.eop("dve", lambda v, d=dstap, s=srcap: v.tensor_copy(d, s),
                                    deps=ps_rdeps(PS_T7, 512) + wdep)
                        ps_setr(PS_T7, 512, mc)
                        lx1_done.append(mc)
                R["Lx1fm"].w = lx1_done[-1]
                for mm_ in lx1_done:
                    R["Lx1fm"].r.append(mm_)
                R["Lx1fm"].r = []
                lx1_all = list(lx1_done)

                kq_loads = []
                for kc in range(16):
                    slot = kc % 2
                    sl = A_KQP + slot * 1024
                    dd = rwrite(R[f"kqp{slot}"], None) + [R["static"].w]
                    if g == 0 and kc == 0:
                        dd += rpeek(R["hsb"])
                    mld = sc.op("pool", lambda g_, kc=kc, sl=sl: g_.dma_start(
                        out=arena[:, sl:sl + 1024], in_=wkqv_d[kc]),
                        deps=dd, inc="DG", n=16)
                    R[f"kqp{slot}"].w = mld
                    kq_loads.append(mld)
                    rot = ROT[kc % 2]
                    def kq_mms(t, sl=sl, rot=rot):
                        ins = None
                        for dc in range(8):
                            ins = t.matmul(
                                psA[:, rot:rot + 512],
                                arena[:, sl + dc * 128:sl + (dc + 1) * 128],
                                arena[:, A_LX1 + dc * 512:A_LX1 + (dc + 1) * 512],
                                start=(dc == 0), stop=(dc == 7))
                        return ins
                    mmk = sc.eop("pe", kq_mms,
                                 deps=[mld, R["Lx1fm"].w] + ps_wdeps(rot, 512))
                    R[f"kqp{slot}"].r.append(mmk)
                    R["Lx1fm"].r.append(mmk)
                    ps_setw(rot, 512, mmk)
                    if kc < 8:
                        dstap = arena[:, A_K + kc * 1024 + g4 * 128:
                                      A_K + kc * 1024 + g4 * 128 + 512]
                        wd = rwrite(R["Kbuf"], None) if (kc == 0 and g == 0) else []
                        mev = sc.eop("dve", lambda v, d=dstap, rot=rot: v.tensor_copy(
                            d, psA[:, rot:rot + 512]), deps=ps_rdeps(rot, 512) + wd)
                    else:
                        qc = kc - 8
                        dstap = arena[:, A_Q + qc * 512:A_Q + qc * 512 + 512]
                        wd = rwrite(R["Qbuf"], None) if kc == 8 else []
                        mev = sc.eop("dve", lambda v, d=dstap, rot=rot: v.tensor_scalar_mul(
                            d, psA[:, rot:rot + 512], SCALE),
                            deps=ps_rdeps(rot, 512) + wd)
                    ps_setr(rot, 512, mev)
                    if kc < 8:
                        R["Kbuf"].w = mev
                    else:
                        R["Qbuf"].w = mev
                vp_deps = rwrite(R["VP"], None) + [R["static"].w]
                vload = []
                for vp in range(8):
                    mld = sc.op("pool", lambda g_, vp=vp: g_.dma_start(
                        out=arena[:, A_VP + vp * 1024:A_VP + (vp + 1) * 1024],
                        in_=wkqv_d[16 + vp]),
                        deps=vp_deps if vp == 0 else [], inc="DG", n=16)
                    vload.append(mld)
                R["VP"].w = vload[-1]
                for tl in range(4):
                    tc = g4 + tl
                    for vh in range(2):
                        rot = ROT[(tl * 2 + vh) % 2]
                        def v_mms(t, tl=tl, vh=vh, rot=rot):
                            ins = None
                            for n_ in range(4):
                                for dc in range(8):
                                    ins = t.matmul(
                                        psA[:, rot + n_ * 128:rot + (n_ + 1) * 128],
                                        arena[:, A_LX1 + dc * 512 + tl * 128:
                                              A_LX1 + dc * 512 + (tl + 1) * 128],
                                        arena[:, A_VP + (vh * 4 + n_) * 1024 + dc * 128:
                                              A_VP + (vh * 4 + n_) * 1024 + (dc + 1) * 128],
                                        start=(dc == 0), stop=(dc == 7))
                            return ins
                        mmv = sc.eop("pe", v_mms,
                                     deps=[vload[vh * 4 + 3], R["Lx1fm"].w]
                                     + ps_wdeps(rot, 512))
                        R["VP"].r.append(mmv)
                        R["Lx1fm"].r.append(mmv)
                        ps_setw(rot, 512, mmv)
                        dstap = arena[:, A_V + tc * 1024 + vh * 512:
                                      A_V + tc * 1024 + (vh + 1) * 512]
                        wd = rwrite(R["Vbuf"], None) if (g == 0 and tl == 0 and vh == 0) else []
                        mev = sc.eop("dve", lambda v, d=dstap, rot=rot: v.tensor_copy(
                            d, psA[:, rot:rot + 512]), deps=ps_rdeps(rot, 512) + wd)
                        ps_setr(rot, 512, mev)
                        R["Vbuf"].w = mev

                if g == 1:
                    kd = rwrite(R["ktail"], None)
                    mkt = sc.eop("dve", lambda v: v.tensor_copy(
                        ktail[:, :].rearrange("p (fc t) -> p fc t", fc=8),
                        arena[:, A_K:A_K + 8192].rearrange(
                            "p (fc t) -> p fc t", fc=8)[:, :, 512:1024]),
                        deps=kd + [R["Kbuf"].w])
                    R["ktail"].w = mkt
                    R["Kbuf"].r.append(mkt)
                    vd = rwrite(R["vtail"], None)
                    mvt = sc.eop("dve", lambda v: v.tensor_copy(
                        vtail[:, :], arena[:, A_V + 4096:A_V + 8192]),
                        deps=vd + [R["Vbuf"].w])
                    R["vtail"].w = mvt
                    R["Vbuf"].r.append(mvt)

                for tl in range(4):
                    tc = g4 + tl
                    gt = c * 8 + tc
                    nw = min(gt, 4) + 1
                    ws0 = 5 - nw
                    par = tc % 2
                    base = par * 48
                    LAG = 3
                    mexp_l = [None] * H
                    for hh in range(H + LAG):
                        if hh < H:
                            h = hh
                            hp = 64 * (h % 2)
                            fcq = h // 2
                            sreg = SREG[h % 3]
                            def qk(t, tl=tl, hp=hp, fcq=fcq, nw=nw, ws0=ws0,
                                   tc=tc, sreg=sreg):
                                ins = None
                                for w in range(nw):
                                    j_c = tc - (nw - 1) + w
                                    if j_c >= 0:
                                        kap = arena[hp:hp + 64,
                                                    A_K + fcq * 1024 + j_c * 128:
                                                    A_K + fcq * 1024 + (j_c + 1) * 128]
                                    else:
                                        kap = ktail[hp:hp + 64,
                                                    fcq * 512 + (j_c + 4) * 128:
                                                    fcq * 512 + (j_c + 5) * 128]
                                    qap = arena[hp:hp + 64,
                                                A_Q + fcq * 512 + tl * 128:
                                                A_Q + fcq * 512 + (tl + 1) * 128]
                                    cs = sreg + (ws0 + w) * 128
                                    ins = t.matmul(psA[:, cs:cs + 128], kap, qap,
                                                   start=True, stop=True)
                                return ins
                            dd = [R["Qbuf"].w, R["Kbuf"].w]
                            if gt >= 4 and nw - 1 > tc:
                                dd.append(R["ktail"].w)
                            dd += ps_wdeps(sreg + ws0 * 128, nw * 128)
                            mqk = sc.eop("pe", qk, deps=dd)
                            R["Qbuf"].r.append(mqk)
                            R["Kbuf"].r.append(mqk)
                            if nw - 1 > tc:
                                R["ktail"].r.append(mqk)
                            ps_setw(sreg + ws0 * 128, nw * 128, mqk)
                            sres = R[f"sst{h % 2}"]
                            sl = ALIBI[h]
                            sap = sstage[:, (h % 2) * 640 + ws0 * 128:
                                         (h % 2) * 640 + 640]
                            wd = rwrite(sres, None)
                            mstt = sc.eop("dve", lambda v, sap=sap, sl=sl, ws0=ws0,
                                          sreg=sreg: v.scalar_tensor_tensor(
                                              sap, rel[:, ws0 * 128:640], sl,
                                              psA[:, sreg + ws0 * 128:sreg + 640],
                                              ALU.mult, ALU.add),
                                          deps=wd + ps_rdeps(sreg + ws0 * 128,
                                                             nw * 128)
                                          + [R["static"].w])
                            sres.w = mstt
                            ps_setr(sreg + ws0 * 128, nw * 128, mstt)
                            pres = R[f"ptb{h % 4}"]
                            pap = pTsb[:, (h % 4) * 640 + ws0 * 128:
                                       (h % 4) * 640 + 640]
                            wd = rwrite(pres, None)
                            mexp = sc.eop("act", lambda a, pap=pap, sap=sap:
                                          a.activation(pap, sap, AF.Exp),
                                          deps=[mstt] + wd)
                            pres.w = mexp
                            sres.r.append(mexp)
                            mexp_l[h] = mexp
                        if hh >= LAG:
                            h = hh - LAG
                            pres = R[f"ptb{h % 4}"]
                            rcp = smx[:, h:h + 1]
                            def rsmm(t, h=h, nw=nw, ws0=ws0):
                                ins = None
                                for w in range(nw):
                                    ins = t.matmul(
                                        psA[:, PS_RS + h:PS_RS + h + 1],
                                        pTsb[:, (h % 4) * 640 + (ws0 + w) * 128:
                                             (h % 4) * 640 + (ws0 + w + 1) * 128],
                                        ones_b[:, :],
                                        start=(w == 0), stop=(w == nw - 1))
                                return ins
                            mrs = sc.eop("pe", rsmm, deps=[mexp_l[h], R["static"].w]
                                         + ps_wdeps(PS_RS + h, 1))
                            pres.r.append(mrs)
                            ps_setw(PS_RS + h, 1, mrs)
                            pvc = PS_PVC[h % 2]
                            def pv(t, h=h, nw=nw, ws0=ws0, tc=tc, pvc=pvc):
                                ins = None
                                for w in range(nw):
                                    j_c = tc - (nw - 1) + w
                                    if j_c >= 0:
                                        vap = arena[:, A_V + j_c * 1024 + h * 64:
                                                    A_V + j_c * 1024 + h * 64 + 64]
                                    else:
                                        vap = vtail[:, (j_c + 4) * 1024 + h * 64:
                                                    (j_c + 4) * 1024 + h * 64 + 64]
                                    ins = t.matmul(
                                        psA[:, pvc:pvc + 64],
                                        pTsb[:, (h % 4) * 640 + (ws0 + w) * 128:
                                             (h % 4) * 640 + (ws0 + w + 1) * 128],
                                        vap, start=(w == 0), stop=(w == nw - 1))
                                return ins
                            dd = [mexp_l[h], R["Vbuf"].w] + ps_wdeps(pvc, 64)
                            if nw - 1 > tc:
                                dd.append(R["vtail"].w)
                            mpv = sc.eop("pe", pv, deps=dd)
                            pres.r.append(mpv)
                            R["Vbuf"].r.append(mpv)
                            if nw - 1 > tc:
                                R["vtail"].r.append(mpv)
                            ps_setw(pvc, 64, mpv)
                            dd = [mrs] if prev_ev[h] is None else [mrs, prev_ev[h]]
                            mrc = sc.eop("dve", lambda v, rcp=rcp, h=h:
                                         v.reciprocal(rcp, psA[:, PS_RS + h:
                                                               PS_RS + h + 1]),
                                         deps=dd + ps_rdeps(PS_RS + h, 1))
                            ps_setr(PS_RS + h, 1, mrc)
                            ares = R["attnsb"]
                            wd = rwrite(ares, None) if h == 0 else []
                            mev = sc.eop("act", lambda a, h=h, rcp=rcp, pvc=pvc:
                                         a.activation(attnsb[:, h * 64:(h + 1) * 64],
                                                      psA[:, pvc:pvc + 64],
                                                      AF.Identity, scale=rcp),
                                         deps=[mrc] + ps_rdeps(pvc, 64) + wd)
                            ps_setr(pvc, 64, mev)
                            ares.w = mev
                            prev_ev[h] = mev

                    rt = resid[:, tc * 1024:(tc + 1) * 1024]
                    ddr = rwrite(Rtile[tc], None)
                    m_r1 = sc.eop("dve", lambda v, rt=rt: v.tensor_add(
                        rt, rt, attnsb[:, :]), deps=ddr + [R["attnsb"].w])
                    Rtile[tc].w = m_r1
                    R["attnsb"].r.append(m_r1)
                    wd = rwrite(R["lx2scr"], None)
                    r2 = emit_ln(rt, m_r1, base, lx2scr[:, :], lx2scr[:, :],
                                 extra_out_deps=wd)
                    R["lx2scr"].w = r2["out"]
                    Rtile[tc].r.append(r2["out"])
                    ddr = rwrite(Rtile[tc], None)
                    m_r15 = sc.eop("dve", lambda v, rt=rt: v.tensor_add(
                        rt, rt, lx2scr[:, :]), deps=ddr + [r2["out"]])
                    Rtile[tc].w = m_r15
                    R["lx2scr"].r.append(m_r15)
                    e3 = stats[:, base + 34:base + 35]
                    t3 = stats[:, base + 35:base + 36]
                    l3 = stats[:, base + 36:base + 37]
                    rstd3 = stats[:, base + 37:base + 38]
                    sd3 = stats[:, base + 38:base + 39]
                    m_e3 = sc.eop("act", lambda a, e3=e3, lv=r2["lnv"]: a.activation(
                        e3, lv, AF.Exp, scale=-1.0), deps=[r2["lnv_m"]])
                    m_t3 = sc.eop("dve", lambda v, t3=t3, e3=e3: v.tensor_scalar(
                        t3, e3, -EPS, 1.0 + EPS, ALU.mult, ALU.add), deps=[m_e3])
                    m_l3 = sc.eop("act", lambda a, l3=l3, t3=t3: a.activation(
                        l3, t3, AF.Ln), deps=[m_t3])
                    m_r3 = sc.eop("act", lambda a, rstd3=rstd3, l3=l3: a.activation(
                        rstd3, l3, AF.Exp, scale=-0.5))
                    m_s3 = sc.eop("act", lambda a, sd3=sd3, l3=l3: a.activation(
                        sd3, l3, AF.Exp, scale=0.5))
                    wd = rwrite(R["xn3tm"], None)
                    m_x3 = sc.eop("dve", lambda v, rstd3=rstd3: v.tensor_scalar_mul(
                        xn3tm[:, :], lx2scr[:, :], rstd3),
                        deps=[m_r3, R["lx2scr"].w] + wd)
                    R["xn3tm"].w = m_x3
                    R["lx2scr"].r.append(m_x3)
                    cps = []
                    for hv in range(2):
                        def tr4(t, hv=hv):
                            ins = None
                            for k in range(4):
                                kk = hv * 4 + k
                                ins = t.transpose(
                                    psA[:, PS_T7 + k * 128:PS_T7 + (k + 1) * 128],
                                    xn3tm[:, kk * 128:(kk + 1) * 128], identf[:, :])
                            return ins
                        mt = sc.eop("pe", tr4, deps=[R["xn3tm"].w]
                                    + ps_wdeps(PS_T7, 512))
                        R["xn3tm"].r.append(mt)
                        ps_setw(PS_T7, 512, mt)
                        srcap = psA[:, PS_T7:PS_T7 + 512].rearrange(
                            "p (k t) -> p k t", k=4)
                        dst1 = xn3fm[:, :].rearrange("p (dc t) -> p dc t", dc=8)[
                            :, hv * 4:(hv + 1) * 4, tc * 128:(tc + 1) * 128]
                        wd = rwrite(R["xn3fm"], None) if (tc == 0 and hv == 0) else []
                        mc1 = sc.eop("dve", lambda v, d=dst1, s=srcap: v.tensor_copy(d, s),
                                     deps=ps_rdeps(PS_T7, 512) + wd)
                        R["xn3fm"].w = mc1
                        wd = rwrite(R["xn3f32"], None) if hv == 0 else []
                        dst2 = xn3f32[:, hv * 512:(hv + 1) * 512]
                        mc2 = sc.eop("dve", lambda v, d=dst2, s=psA[:, PS_T7:PS_T7 + 512]:
                                     v.tensor_copy(d, s),
                                     deps=ps_rdeps(PS_T7, 512) + wd)
                        R["xn3f32"].w = mc2
                        ps_setr(PS_T7, 512, mc1)
                        ps_setr(PS_T7, 512, mc2)
                        cps.append(mc2)
                    def aff_mm(t):
                        ins = None
                        for dc in range(8):
                            ins = t.matmul(psA[:, PS_AFF:PS_AFF + 8],
                                           xn3f32[:, dc * 128:(dc + 1) * 128],
                                           centc[:, dc * 8:(dc + 1) * 8],
                                           start=(dc == 0), stop=(dc == 7))
                        return ins
                    maff = sc.eop("pe", aff_mm, deps=[R["xn3f32"].w, R["centc"].w]
                                  + ps_wdeps(PS_AFF, 8))
                    R["xn3f32"].r.append(maff)
                    ps_setw(PS_AFF, 8, maff)
                    affs = stats[:, base + 17:base + 25]
                    m_t = sc.eop("dve", lambda v, affs=affs, sdap=sd3:
                                 v.tensor_scalar_mul(
                                     affs, psA[:, PS_AFF:PS_AFF + 8], sdap),
                                 deps=[m_s3] + ps_rdeps(PS_AFF, 8))
                    ps_setr(PS_AFF, 8, m_t)
                    sc.op("dve", lambda v: v.drain())
                    mxa = stc(base, "maxa")
                    m_mx = sc.eop("dve", lambda v, affs=affs, mxa=mxa: v.reduce_max(
                        mxa, affs, axis=AX.X))
                    sc.op("dve", lambda v: v.drain())
                    ohap = onehot[:, tc * 8:(tc + 1) * 8]
                    wd = rwrite(R["onehot"], None) if tc == 0 else []
                    m_oh = sc.eop("dve", lambda v, ohap=ohap, affs=affs, mxa=mxa:
                                  v.tensor_scalar(ohap, affs, mxa, None, ALU.is_equal),
                                  deps=wd)
                    R["onehot"].w = m_oh
                    eal = stc(base, "ealpha")
                    m_e = sc.eop("act", lambda a, eal=eal, mxa=mxa: a.activation(
                        eal, mxa, AF.Exp, scale=-1.0), deps=[m_mx])
                    tal = stc(base, "talpha")
                    m_t2 = sc.eop("dve", lambda v, tal=tal, eal=eal: v.tensor_scalar_add(
                        tal, eal, 1.0), deps=[m_e])
                    sc.op("dve", lambda v: v.drain())
                    alap = alpha[:, tc:tc + 1]
                    wd = rwrite(R["alpha"], None) if tc == 0 else []
                    m_al = sc.eop("dve", lambda v, alap=alap, tal=tal: v.reciprocal(
                        alap, tal), deps=wd)
                    R["alpha"].w = m_al
                    sc.op("dve", lambda v: v.drain())
                    amap = amask[:, tc * 8:(tc + 1) * 8]
                    wd = rwrite(R["amask"], None) if tc == 0 else []
                    m_am = sc.eop("dve", lambda v, amap=amap, ohap=ohap, alap=alap:
                                  v.tensor_scalar_mul(amap, ohap, alap),
                                  deps=wd + [m_oh])
                    R["amask"].w = m_am
                    m_ot = sc.eop("pe", lambda t, ohap=ohap: t.transpose(
                        psA[0:8, PS_ONET:PS_ONET + 128], ohap, identf[:, :]),
                        deps=[m_oh] + ps_wdeps(PS_ONET, 128))
                    R["onehot"].r.append(m_ot)
                    ps_setw(PS_ONET, 128, m_ot)
                    otap = oneT[0:8, tc * 128:(tc + 1) * 128]
                    wd = rwrite(R["oneT"], None) if tc == 0 else []
                    m_otc = sc.eop("dve", lambda v, otap=otap: v.tensor_copy(
                        otap, psA[0:8, PS_ONET:PS_ONET + 128]),
                        deps=ps_rdeps(PS_ONET, 128) + wd)
                    R["oneT"].w = m_otc
                    ps_setr(PS_ONET, 128, m_otc)

            for e in range(NE):
                for th in range(2):
                    ti = (e * 2 + th)
                    slot = ti % 2
                    xres = R[f"xe{slot}"]
                    wd = rwrite(xres, None)
                    xe0 = A_XE + slot * 4096
                    m_xe = None
                    for dc in range(8):
                        src = xn3fm[:, dc * 1024 + th * 512:dc * 1024 + (th + 1) * 512]
                        dst = arena[:, xe0 + dc * 512:xe0 + (dc + 1) * 512]
                        gap = gc[:, dc * 8 + e:dc * 8 + e + 1]
                        bap = bc[:, dc * 8 + e:dc * 8 + e + 1]
                        m_xe = sc.eop("act", lambda a, dst=dst, src=src, gap=gap,
                                      bap=bap: a.activation(dst, src, AF.Identity,
                                                            bias=bap, scale=gap),
                                      deps=(wd + [R["xn3fm"].w, R["gc"].w, R["bc"].w])
                                      if dc == 0 else [])
                    xres.w = m_xe
                    R["xn3fm"].r.append(m_xe)
                    relus = []
                    for fcg in range(8):
                        wslot = fcg % 3
                        w1r = R[f"w1{wslot}"]
                        wd = rwrite(w1r, None)
                        if e == 0 and th == 0 and fcg == 0:
                            wd += rpeek(R["Vbuf"], R["Qbuf"], R["VP"])
                        w1base = A_W1 + wslot * 4096
                        src = w1_d[e, fcg]
                        dst = arena[:, w1base:w1base + 4096]
                        mld = sc.op("pool", lambda g_, dst=dst, src=src: g_.dma_start(
                            out=dst, in_=src), deps=wd, inc="DG", n=16)
                        w1r.w = mld
                        for fc4 in range(4):
                            fc = fcg * 4 + fc4
                            rot = ROT[fc % 2]
                            def hmm(t, w1base=w1base, fc4=fc4, xe0=xe0, rot=rot):
                                ins = None
                                for dc in range(8):
                                    ins = t.matmul(
                                        psA[:, rot:rot + 512],
                                        arena[:, w1base + fc4 * 1024 + dc * 128:
                                              w1base + fc4 * 1024 + (dc + 1) * 128],
                                        arena[:, xe0 + dc * 512:xe0 + (dc + 1) * 512],
                                        start=(dc == 0), stop=(dc == 7))
                                return ins
                            mmh = sc.eop("pe", hmm, deps=[mld, xres.w]
                                         + ps_wdeps(rot, 512))
                            w1r.r.append(mmh)
                            xres.r.append(mmh)
                            ps_setw(rot, 512, mmh)
                            hres = R["hsb"]
                            wd2 = rwrite(hres, None) if fc == 0 else []
                            hdst = arena[:, A_H + fc * 512:A_H + (fc + 1) * 512]
                            b1ap = b1c[:, fc * 8 + e:fc * 8 + e + 1]
                            mrl = sc.eop("act", lambda a, hdst=hdst, b1ap=b1ap, rot=rot:
                                         a.activation(hdst, psA[:, rot:rot + 512],
                                                      AF.Relu, bias=b1ap),
                                         deps=ps_rdeps(rot, 512) + wd2 + [R["b1c"].w])
                            ps_setr(rot, 512, mrl)
                            hres.w = mrl
                            relus.append(mrl)
                    evacs = {}
                    for fcg in range(8):
                        wslot = fcg % 4
                        w2r = R[f"w2{wslot}"]
                        wd = rwrite(w2r, None)
                        w2base = A_W2 + wslot * 4096
                        src = w2_d[e, fcg]
                        dst = arena[:, w2base:w2base + 4096]
                        mld = sc.op("pool", lambda g_, dst=dst, src=src: g_.dma_start(
                            out=dst, in_=src), deps=wd, inc="DG", n=16)
                        w2r.w = mld
                        for sub in range(4):
                            for dh in range(2):
                                co = PSO[(sub, dh)]
                                def omm(t, w2base=w2base, fcg=fcg, sub=sub, dh=dh, co=co):
                                    ins = None
                                    for fc4 in range(4):
                                        fc = fcg * 4 + fc4
                                        ins = t.matmul(
                                            psA[:, co:co + 512],
                                            arena[:, A_H + fc * 512 + sub * 128:
                                                  A_H + fc * 512 + (sub + 1) * 128],
                                            arena[:, w2base + fc4 * 1024 + dh * 512:
                                                  w2base + fc4 * 1024 + (dh + 1) * 512],
                                            start=(fcg == 0 and fc4 == 0),
                                            stop=(fcg == 7 and fc4 == 3))
                                    return ins
                                dd = [mld, relus[fcg * 4 + 3]]
                                if fcg == 0:
                                    dd += ps_wdeps(co, 512)
                                mo = sc.eop("pe", omm, deps=dd)
                                w2r.r.append(mo)
                                R["hsb"].r.append(mo)
                                if fcg == 0:
                                    ps_setw(co, 512, mo)
                                else:
                                    for b in ps_banks(co, 512):
                                        bank[b].w = mo
                                if fcg == 7:
                                    evacs[(sub, dh)] = mo
                    for sub in (3, 0, 1, 2):
                        tile_c = th * 4 + sub
                        amcol = amask[:, tile_c * 8 + e:tile_c * 8 + e + 1]
                        for dh in range(2):
                            co = PSO[(sub, dh)]
                            adst = resid[:, tile_c * 1024 + dh * 512:
                                         tile_c * 1024 + (dh + 1) * 512]
                            mev = sc.eop("dve", lambda v, adst=adst, amcol=amcol,
                                         co=co: v.scalar_tensor_tensor(
                                             adst, psA[:, co:co + 512], amcol,
                                             adst, ALU.mult, ALU.add),
                                         deps=[evacs[(sub, dh)], R["amask"].w,
                                               Rtile[tile_c].w])
                            R["amask"].r.append(mev)
                            ps_setr(co, 512, mev)
                            Rtile[tile_c].w = mev

            for tc in range(8):
                otap = oneT[0:8, tc * 128:(tc + 1) * 128]
                mmb = []
                for dh in range(2):
                    co = PS_B2[dh]
                    m_ = sc.eop("pe", lambda t, otap=otap, dh=dh, co=co: t.matmul(
                        psA[:, co:co + 512], otap, b2sb[0:8, dh * 512:(dh + 1) * 512],
                        start=True, stop=True),
                        deps=[R["oneT"].w, R["b2sb"].w] + ps_wdeps(co, 512))
                    R["oneT"].r.append(m_)
                    ps_setw(co, 512, m_)
                    mmb.append(m_)
                rt = resid[:, tc * 1024:(tc + 1) * 1024]
                alap = alpha[:, tc:tc + 1]
                souts = []
                for dh in range(2):
                    co = PS_B2[dh]
                    m_ = sc.eop("dve", lambda v, dh=dh, co=co, rt=rt, alap=alap:
                                v.scalar_tensor_tensor(
                                    rt[:, dh * 512:(dh + 1) * 512],
                                    psA[:, co:co + 512], alap,
                                    rt[:, dh * 512:(dh + 1) * 512],
                                    ALU.mult, ALU.add),
                                deps=[mmb[dh], R["alpha"].w, Rtile[tc].w])
                    ps_setr(co, 512, m_)
                    R["alpha"].r.append(m_)
                    souts.append(m_)
                Rtile[tc].w = souts[-1]
                ydst = y_d[(c * 8 + tc) * 128:(c * 8 + tc + 1) * 128, :]
                m_o = sc.op("sp", lambda s, ydst=ydst, rt=rt: s.dma_start(
                    out=ydst, in_=rt), deps=souts, inc="DO", n=16)
                Rtile[tc].r.append(m_o)

        total_out = sc.C["DO"]
        sc.op("sp", lambda s: s.nop(), deps=[("DO", total_out)])

        def play(e, name):
            for deps, fn, inc, n in sc.q[name]:
                for s_, v_ in deps:
                    e.wait_ge(sems[s_], v_)
                ins = fn(e)
                if inc is not None and ins is not None:
                    ins.then_inc(sems[inc], n)

        @block.tensor
        def _(t):
            play(t, "pe")

        @block.scalar
        def _(a):
            play(a, "act")

        @block.vector
        def _(v):
            play(v, "dve")

        @block.gpsimd
        def _(g):
            play(g, "pool")

        @block.sync
        def _(s):
            play(s, "sp")

    return nc


def kernel(L0_attn_x, L0_attn_w_kqv, L0_ffn_centroids, L0_ffn_ln_g, L0_ffn_ln_b,
           L0_ffn_w1, L0_ffn_b1, L0_ffn_w2, L0_ffn_b2):
    x = np.ascontiguousarray(np.asarray(L0_attn_x, dtype=np.float32))
    wkqv = np.asarray(L0_attn_w_kqv, dtype=np.float32)
    cent = np.asarray(L0_ffn_centroids, dtype=np.float32)
    ln_g = np.asarray(L0_ffn_ln_g, dtype=np.float32)
    ln_b = np.asarray(L0_ffn_ln_b, dtype=np.float32)
    w1 = np.asarray(L0_ffn_w1, dtype=np.float32)
    b1 = np.asarray(L0_ffn_b1, dtype=np.float32)
    w2 = np.asarray(L0_ffn_w2, dtype=np.float32)
    b2 = np.asarray(L0_ffn_b2, dtype=np.float32)

    wkqv_p = np.ascontiguousarray(
        wkqv.reshape(8, 128, 24, 128).transpose(2, 1, 0, 3))
    wkqv_p = wkqv_p.reshape(24, 128, 1024)
    w1_p = np.ascontiguousarray(
        w1.reshape(NE, 8, 128, 32, 128).transpose(0, 3, 2, 1, 4)
        .reshape(NE, 8, 4, 128, 1024).transpose(0, 1, 3, 2, 4)).reshape(NE, 8, 128, 4096)
    w2_p = np.ascontiguousarray(
        w2.reshape(NE, 8, 4, 128, 1024).transpose(0, 1, 3, 2, 4)).reshape(NE, 8, 128, 4096)
    b1c = np.ascontiguousarray(b1.T.reshape(32, 128, NE).transpose(1, 0, 2)).reshape(128, 256)
    gcq = np.ascontiguousarray(ln_g.T.reshape(8, 128, NE).transpose(1, 0, 2)).reshape(128, 64)
    bcq = np.ascontiguousarray(ln_b.T.reshape(8, 128, NE).transpose(1, 0, 2)).reshape(128, 64)
    centc = np.ascontiguousarray(cent.T.reshape(8, 128, NE).transpose(1, 0, 2)).reshape(128, 64)

    nc = build_program()
    in_maps = []
    for b in range(NCORES):
        in_maps.append({
            "x": np.ascontiguousarray(x[b]),
            "wkqv": wkqv_p, "w1": w1_p, "w2": w2_p,
            "b1c": b1c, "gc": gcq, "bc": bcq, "centc": centc, "b2": b2,
        })
    res = run_bass_kernel_spmd(nc, in_maps, core_ids=list(range(NCORES)))
    out = np.stack([np.asarray(res.results[b]["y"]) for b in range(NCORES)], axis=0)
    return out.astype(np.float32)
